# version 16
# baseline (speedup 1.0000x reference)
import contextlib

import numpy as np

import concourse.bass as bass
import concourse.mybir as mybir
from concourse.bass_utils import run_bass_kernel_spmd

F32 = mybir.dt.float32
BF16 = mybir.dt.bfloat16
AF = mybir.ActivationFunctionType
ALU = mybir.AluOpType
AX = mybir.AxisListType.X

N_CORES = 8
D = 2048
TOWN = 1024
E = 8192
NDC = D // 128
NEC = E // 128
NG = 16
NSLAB = 8
NQT = 16
CLAMP = 75.0
EPS_LN = 1e-5
EPS_Q = 1e-8
NW = float(D * E)


def _roundup(x, a=32):
    return (x + a - 1) // a * a


def build_program(debug=False):
    nc = bass.Bass("TRN2", target_bir_lowering=False,
                   detect_race_conditions=False)

    x_own = nc.dram_tensor("x_own", [D, TOWN], F32, kind="ExternalInput")
    xt_own = nc.dram_tensor("xt_own", [128, 16384], F32, kind="ExternalInput")
    xt_pri = nc.dram_tensor("xt_pri", [128, 16384], F32, kind="ExternalInput")
    gatec = nc.dram_tensor("gatec", [64, 1], F32, kind="ExternalInput")
    wz = nc.dram_tensor("wz", [128, NDC * 384], F32, kind="ExternalInput")
    brow = nc.dram_tensor("brow", [1, 384], F32, kind="ExternalInput")
    wo = nc.dram_tensor("wo", [128, D], F32, kind="ExternalInput")
    tri = nc.dram_tensor("tri", [128, 128], F32, kind="ExternalInput")
    gsel = nc.dram_tensor("gsel", [128, 128], F32, kind="ExternalInput")
    maskc = nc.dram_tensor("maskc", [128, 128], F32, kind="ExternalInput")
    iden = nc.dram_tensor("iden", [128, 128], F32, kind="ExternalInput")
    raw_u = nc.dram_tensor("raw_u", [128, 16384], F32, kind="ExternalInput")
    raw_g = nc.dram_tensor("raw_g", [128, 16384], F32, kind="ExternalInput")
    raw_d = nc.dram_tensor("raw_d", [128, 16384], F32, kind="ExternalInput")
    out = nc.dram_tensor("out", [D, TOWN], F32, kind="ExternalOutput")
    if debug:
        dbg_xn = nc.dram_tensor("dbg_xn", [128, TOWN], F32, kind="ExternalOutput")
        dbg_row6 = nc.dram_tensor("dbg_row6", [1, 6], F32, kind="ExternalOutput")
        dbg_yp = nc.dram_tensor("dbg_yp", [D, TOWN], F32, kind="ExternalOutput")
        dbg_S = nc.dram_tensor("dbg_S", [64, 128], F32, kind="ExternalOutput")
        dbg_h = nc.dram_tensor("dbg_h", [128, TOWN], F32, kind="ExternalOutput")
        dbg_w = nc.dram_tensor("dbg_w", [128, 4096], F32, kind="ExternalOutput")

    ar_in = nc.dram_tensor("ar_in", [1, 4], F32)
    ar_out = nc.dram_tensor("ar_out", [1, 4], F32)
    ag_in = [nc.dram_tensor(f"ag_in{m}", [128, 16384], BF16) for m in range(3)]
    ag_out = [nc.dram_tensor(f"ag_out{m}", [128, 131072], BF16) for m in range(3)]
    y_sc = nc.dram_tensor("y_sc", [D, TOWN], F32)

    RG = [list(range(N_CORES))]

    ARENA_BYTES = 209920
    SB_TOP = ARENA_BYTES
    arena = nc.alloc_sbuf_tensor("arena", [128, ARENA_BYTES], mybir.dt.uint8)
    _off = [0]
    _offmap = {}

    def alloc(name, shape, dt, at=None):
        if at is not None:
            _off[0] = at
        per = 1
        for s in shape[1:]:
            per *= s
        per *= mybir.dt.size(dt)
        ap = arena.ap()[0:shape[0], _off[0]:_off[0] + per].bitcast(dt)
        _offmap[name] = _off[0]
        _off[0] += _roundup(per)
        return ap

    tri_s = alloc("tri_s", [128, 128], F32)
    gsel_s = alloc("gsel_s", [128, 128], F32)
    mask_s = alloc("mask_s", [128, 128], F32)
    iden_s = alloc("iden_s", [128, 128], F32)
    gate_s = alloc("gate_s", [64, 1], F32)
    wo_b = alloc("wo_b", [128, D], BF16)
    brow_b = alloc("brow_b", [1, 384], BF16)
    onesr_b = alloc("onesr_b", [1, 128], BF16)
    onesc_b = alloc("onesc_b", [128, 1], BF16)
    onesr_f = alloc("onesr_f", [1, 128], F32)
    onesc_f = alloc("onesc_f", [128, 1], F32)
    colbuf = alloc("colbuf", [128, 6], F32)
    ncolbuf = alloc("ncolbuf", [128, 3], F32)
    row6 = alloc("row6", [1, 6], F32)
    arbuf = alloc("arbuf", [1, 4], F32)
    arsb = alloc("arsb", [1, 4], F32)
    Sst = alloc("Sst", [64, 128], F32)
    Sb = alloc("Sb", [64, 128], BF16)
    xn_own = alloc("xn_own", [128, NDC * TOWN], BF16)
    PERS_END = _off[0]

    A0 = PERS_END
    xn_pri = alloc("xn_pri", [128, NDC * TOWN], BF16, at=A0)
    xq = alloc("xq", [128, NDC * 256], BF16)
    x2q = alloc("x2q", [128, NDC * 256], BF16)
    wz_b = alloc("wz_b", [128, NDC * 384], BF16)
    qraw = [alloc(f"qraw{i}", [128, 1024], F32) for i in range(2)]
    qtile = [alloc(f"qtile{i}", [128, 1024], BF16) for i in range(2)]
    gq1 = alloc("gq1", [128, 1024], BF16)
    gq2 = alloc("gq2", [128, 1024], BF16)
    pcols = alloc("pcols", [128, 48], F32)
    scol = alloc("scol", [128, 3], F32)
    mrow = alloc("mrow", [1, 256], F32)
    qrow = alloc("qrow", [1, 256], F32)
    msq = alloc("msq", [1, 256], F32)
    varr = alloc("varr", [1, 256], F32)
    rstdr = alloc("rstdr", [1, 256], F32)
    mrr = alloc("mrr", [1, 256], F32)
    rstdb = alloc("rstdb", [1, 256], BF16)
    mrb = alloc("mrb", [1, 256], BF16)
    rbuf = alloc("rbuf", [128, 256], BF16)
    mbuf = alloc("mbuf", [128, 256], BF16)
    tbs = alloc("tbs", [128, 256], BF16)
    sgb = [alloc(f"sgb{g}", [128, 64], F32) for g in range(NG)]
    sgd = [alloc(f"sgd{g}", [128, 64], F32) for g in range(NG)]
    vbuf = [alloc(f"vbuf{g}", [128, 128], BF16) for g in range(NG)]
    zkb = [alloc(f"zkb{g}", [128, 64], F32) for g in range(NG)]
    qbuf = [alloc(f"qbuf{g}", [128, 64], F32) for g in range(NG)]
    acb = [alloc(f"acb{g}", [64, 2], F32) for g in range(NG)]
    khat = [alloc(f"khat{g}", [128, 64], BF16) for g in range(NG)]
    ktil = [alloc(f"ktil{g}", [64, 128], BF16) for g in range(NG)]
    qtl = [alloc(f"qtl{g}", [64, 128], BF16) for g in range(8)]
    laq = alloc("laq", [128, 128], F32)
    qtk = alloc("qtk", [128, 128], F32)
    ktk = alloc("ktk", [128, 128], F32)
    atok = alloc("atok", [128, 64], F32)
    entok = alloc("entok", [128, 64], F32)
    ectok = alloc("ectok", [128, 64], F32)
    dfs = alloc("dfs", [128, 64], F32)
    t1b = alloc("t1b", [128, 64], F32)
    t2b = alloc("t2b", [128, 64], F32)
    pmb = alloc("pmb", [128, 128], BF16)
    Ob = alloc("Ob", [128, TOWN], BF16)
    xo = [alloc(f"xo{i}", [128, 512], F32) for i in range(2)]
    yst = [alloc(f"yst{i}", [128, 512], F32) for i in range(2)]
    A_END = _off[0]

    hbuf = alloc("hbuf", [128, NEC * TOWN], BF16, at=A0)
    B1 = _off[0]
    bufU = [alloc(f"bufU{i}", [128, 2048], BF16) for i in range(2)]
    bufG = [alloc(f"bufG{i}", [128, 2048], BF16) for i in range(2)]
    B2 = _off[0]
    bufD = [alloc(f"bufD{i}", [128, 4096], BF16, at=B1 if i == 0 else None)
            for i in range(2)]
    _off[0] = B2
    sgf = alloc("sgf", [128, 1024], F32, at=B2)
    ybuf = [alloc(f"ybuf{i}", [128, 1024], F32) for i in range(2)]
    ostage = [alloc(f"ostage{i}", [128, 1024], F32) for i in range(2)]
    B_END = _off[0]
    assert A_END <= SB_TOP and B_END <= SB_TOP, (PERS_END, A_END, B_END)
    assert B1 >= A_END, (B1, A_END)

    GUARD_QT_EC = max(0, (_offmap["qraw0"] - A0) // 2048 - 1)
    GUARD_YS_EC = max(0, (_offmap["Ob"] - A0) // 2048 - 1)

    es = contextlib.ExitStack()
    with es:
        def sem(name):
            return es.enter_context(nc.semaphore(name))

        pb = [es.enter_context(nc.psum_tensor(f"pb{i}", [128, 512], F32))
              for i in range(8)]

        sLd = sem("sLd"); sGc = sem("sGc")
        sQr = [sem("sQrA"), sem("sQrB")]
        sVq1 = sem("sVq1"); sVq2 = sem("sVq2")
        sTq = sem("sTq"); sTb = sem("sTb")
        sCCar = sem("sCCar"); sCCu = sem("sCCu"); sCCg = sem("sCCg")
        sCCd = sem("sCCd")
        sGar = sem("sGar"); sStq = sem("sStq")
        sFWg = [sem("sFWg0"), sem("sFWg1")]
        sGx = sem("sGx"); sLNs = sem("sLNs"); sLNt = sem("sLNt"); sLNv = sem("sLNv")
        sZt = sem("sZt"); sZv = sem("sZv"); sZs = sem("sZs")
        sA2t = sem("sA2t"); sA2v = sem("sA2v"); sA2s = sem("sA2s")
        sA3t = sem("sA3t"); sA3v = sem("sA3v")
        sXop = [sem("sXopA"), sem("sXopB")]
        sOPt = sem("sOPt"); sOPv = sem("sOPv"); sStY = sem("sStY")
        sFW = [sem("sFWA"), sem("sFWB")]
        sFUt = sem("sFUt"); sFUs = sem("sFUs"); sFUv = sem("sFUv")
        sFW2 = [sem("sFW2A"), sem("sFW2B")]
        sYr = [sem("sYrA"), sem("sYrB")]
        sFDt = sem("sFDt"); sFDv = sem("sFDv")
        sSto = sem("sSto"); sDbg = sem("sDbg"); sMisc = sem("sMisc")
        sMiscS = sem("sMiscS")
        sCCb = sem("sCCb")
        _mv = [0]
        _ms = [0]

        t_incs = [2 if g < 8 else 6 for g in range(NG)]
        v_incs = [2 if g < 8 else 4 for g in range(NG)]
        v_incs[7] += 1
        Tst = [0]; Vst = [0]
        for g in range(NG):
            Tst.append(Tst[-1] + t_incs[g])
            Vst.append(Vst[-1] + v_incs[g])
        A3T_TOT, A3V_TOT = Tst[NG], Vst[NG]

        raws = [raw_u, raw_g, raw_d]
        blk = es.enter_context(nc.Block())

        @blk.sync
        def _(sync):
            sync.dma_start(out=tri_s[:], in_=tri.ap()).then_inc(sLd, 16)
            sync.dma_start(out=gsel_s[:], in_=gsel.ap()).then_inc(sLd, 16)
            sync.dma_start(out=mask_s[:], in_=maskc.ap()).then_inc(sLd, 16)
            sync.dma_start(out=iden_s[:], in_=iden.ap()).then_inc(sLd, 16)
            sync.dma_start(out=gate_s[:], in_=gatec.ap()).then_inc(sLd, 16)
            for t in range(3 * NQT):
                m, i = divmod(t, NQT)
                if t >= 2:
                    sync.wait_ge(sVq1, t - 1)
                sync.dma_start(out=qraw[t % 2][:],
                               in_=raws[m].ap()[:, i * 1024:(i + 1) * 1024]
                               ).then_inc(sQr[t % 2], 16)
            sync.wait_ge(sVq1, 3 * NQT)
            for t in range(3 * NQT):
                m, i = divmod(t, NQT)
                if t >= 2:
                    sync.wait_ge(sVq2, (t - 2) + 3)
                sync.dma_start(out=qraw[t % 2][:],
                               in_=raws[m].ap()[:, i * 1024:(i + 1) * 1024]
                               ).then_inc(sQr[t % 2], 16)
            for it in range(32):
                half, dc = divmod(it, NDC)
                if it >= 2:
                    sync.wait_ge(sOPv, it - 1)
                sync.dma_start(out=xo[it % 2][:],
                               in_=x_own.ap()[dc * 128:(dc + 1) * 128,
                                              half * 512:(half + 1) * 512]
                               ).then_inc(sXop[it % 2], 16)
            sync.wait_ge(sCCu, 1)
            sync.wait_ge(sCCg, 1)
            sync.wait_ge(sCCb, 1)
            for ec in range(NEC):
                if ec >= 2:
                    sync.wait_ge(sFUt, 2 * (ec - 2) + 2)
                sync.dma_start(out=bufU[ec % 2][:],
                               in_=ag_out[0].ap()[:, ec * 2048:(ec + 1) * 2048]
                               ).then_inc(sFW[ec % 2], 16)
            sync.wait_ge(sCCd, 1)
            sync.wait_ge(sFUt, 2 * NEC)
            sync.wait_ge(sFUs, NEC)
            sync.wait_ge(sStY, 16 * 32)
            for dc in range(NDC):
                base = dc * 8192
                if dc >= 1:
                    sync.wait_ge(sFDt, 2 * (dc - 1) + 1)
                sync.dma_start(out=bufD[0][:],
                               in_=ag_out[2].ap()[:, base:base + 4096]
                               ).then_inc(sFW2[0], 16)
                if dc >= 2:
                    sync.wait_ge(sFDv, dc - 1)
                sync.dma_start(out=ybuf[dc % 2][:],
                               in_=y_sc.ap()[dc * 128:(dc + 1) * 128, :]
                               ).then_inc(sYr[dc % 2], 16)

        @blk.gpsimd
        def _(g):
            g.dma_start(out=wz_b[:], in_=wz.ap()).then_inc(sGc, 16)
            g.dma_start(out=brow_b[:], in_=brow.ap()).then_inc(sGc, 16)
            g.dma_start(out=wo_b[:], in_=wo.ap()).then_inc(sGc, 16)
            for s in range(NSLAB):
                srcx = xt_pri if s < 4 else xt_own
                c0 = (s % 4) * 4096
                if s >= 1:
                    g.wait_ge(sLNv, 3 * s)
                g.dma_start(out=xq[:],
                            in_=srcx.ap()[:, c0:c0 + 4096]).then_inc(sGx, 16)
            g.wait_ge(sVq1, 3 * NQT + 5)
            g.dma_start(out=ar_in.ap(), in_=arbuf[:]).then_inc(sStq, 16)
            g.wait_ge(sStq, 16)
            g.collective_compute("AllReduce", ALU.add, replica_groups=RG,
                                 ins=[ar_in.ap()], outs=[ar_out.ap()]
                                 ).then_inc(sCCar, 1)
            g.wait_ge(sCCar, 1)
            g.dma_start(out=arsb[:], in_=ar_out.ap()).then_inc(sGar, 16)
            nst = 1
            for t in range(3 * NQT):
                m, i = divmod(t, NQT)
                g.wait_ge(sVq2, min(t + 4, 51))
                g.dma_start(out=ag_in[m].ap()[:, i * 1024:(i + 1) * 1024],
                            in_=qtile[t % 2][:]).then_inc(sStq, 16)
                nst += 1
                if i == NQT - 1:
                    g.wait_ge(sStq, 16 * nst)
                    g.collective_compute("AllGather", ALU.bypass, replica_groups=RG,
                                         ins=[ag_in[m].ap()], outs=[ag_out[m].ap()]
                                         ).then_inc([sCCu, sCCg, sCCd][m], 1)
            g.collective_compute("AllReduce", ALU.add, replica_groups=RG,
                                 ins=[ar_in.ap()], outs=[ar_out.ap()]
                                 ).then_inc(sCCb, 1)
            for it in range(32):
                half, dc = divmod(it, NDC)
                g.wait_ge(sOPv, min(it + 2, 33))
                g.dma_start(out=y_sc.ap()[dc * 128:(dc + 1) * 128,
                                          half * 512:(half + 1) * 512],
                            in_=yst[it % 2][:]).then_inc(sStY, 16)
            if debug:
                g.wait_ge(sStY, 16 * 32)
                g.dma_start(out=dbg_yp.ap(), in_=y_sc.ap()).then_inc(sDbg, 16)
                g.dma_start(out=dbg_xn.ap(), in_=xn_own[:, 0:TOWN]).then_inc(sDbg, 16)
                g.dma_start(out=dbg_row6.ap(), in_=row6[:]).then_inc(sDbg, 16)
                g.dma_start(out=dbg_S.ap(), in_=Sst[:]).then_inc(sDbg, 16)
                g.wait_ge(sDbg, 16 * 4)
            for dc in range(NDC):
                g.wait_ge(sFDv, min(dc + 2, 17))
                g.dma_start(out=out.ap()[dc * 128:(dc + 1) * 128, :],
                            in_=ostage[dc % 2][:]).then_inc(sSto, 16)
            if debug:
                g.dma_start(out=dbg_h.ap(), in_=hbuf[:, 0:TOWN]).then_inc(sDbg, 16)
                g.dma_start(out=dbg_w.ap(), in_=ag_out[0].ap()[:, 0:4096]
                            ).then_inc(sDbg, 16)
                g.wait_ge(sDbg, 16 * 6)
            g.wait_ge(sSto, 16 * NDC)

        @blk.vector
        def _(v):
            v.memset(onesr_b[:], 1.0)
            v.memset(onesc_b[:], 1.0)
            v.memset(onesr_f[:], 1.0)
            v.memset(onesc_f[:], 1.0)
            v.memset(Sst[:], 0.0)
            v.memset(Sb[:], 0.0)
            v.memset(laq[:], 0.0)
            v.memset(qtk[:], 0.0)
            v.memset(ktk[:], 0.0)
            v.wait_ge(sGc, 48)
            for i in range(NDC):
                ins = v.tensor_scalar_mul(wz_b[:, i * 384 + 320:i * 384 + 384],
                                          wz_b[:, i * 384 + 320:i * 384 + 384], -1.0)
            ins.then_inc(sZv, 1)
            for t in range(3 * NQT):
                v.wait_ge(sQr[t % 2], 16 * (t // 2 + 1))
                v.reduce_sum(pcols[:, t:t + 1], qraw[t % 2][:], AX,
                             apply_absolute_value=True).then_inc(sVq1, 1)
            v.wait_ge(sVq1, 3 * NQT)
            for m in range(3):
                v.reduce_sum(scol[:, m:m + 1], pcols[:, m * NQT:(m + 1) * NQT], AX
                             ).then_inc(sVq1, 1)
            v.wait_ge(sTq, 3)
            v.tensor_copy(arbuf[0:1, 0:3], pb[6][0:1, 0:3])
            v.memset(arbuf[0:1, 3:4], 0.0).then_inc(sVq1, 1)
            v.tensor_copy(tbs[0:1, 0:2], arbuf[0:1, 0:2]).then_inc(sVq1, 1)
            for s in range(NSLAB):
                xn_dst = xn_pri if s < 4 else xn_own
                off = (s % 4) * 256
                v.wait_ge(sLNt, 2 * s + 1)
                v.tensor_scalar_mul(mrow[:], pb[0][0:1, 0:256], 1.0 / D)
                v.tensor_scalar_mul(qrow[:], pb[1][0:1, 0:256],
                                    1.0 / D).then_inc(sMisc, 1)
                _mv[0] += 1
                v.wait_ge(sMisc, _mv[0])
                v.tensor_mul(msq[:], mrow[:], mrow[:]).then_inc(sMisc, 1)
                _mv[0] += 1
                v.wait_ge(sMisc, _mv[0])
                v.scalar_tensor_tensor(varr[:], qrow[:], EPS_LN, msq[:],
                                       ALU.add, ALU.subtract).then_inc(sLNv, 1)
                v.wait_ge(sLNs, 2 * s + 2)
                v.tensor_mul(mrr[:], mrow[:], rstdr[:]).then_inc(sMisc, 1)
                _mv[0] += 1
                v.wait_ge(sMisc, _mv[0])
                v.tensor_copy(rstdb[:], rstdr[:])
                v.tensor_copy(mrb[:], mrr[:]).then_inc(sLNv, 1)
                v.wait_ge(sLNt, 2 * s + 2)
                v.tensor_copy(rbuf[:], pb[2][:, 0:256])
                v.tensor_copy(mbuf[:], pb[3][:, 0:256]).then_inc(sMisc, 1)
                _mv[0] += 1
                v.wait_ge(sMisc, _mv[0])
                for i in range(NDC):
                    v.tensor_mul(tbs[:], xq[:, i * 256:(i + 1) * 256],
                                 rbuf[:]).then_inc(sMisc, 1)
                    _mv[0] += 1
                    v.wait_ge(sMisc, _mv[0])
                    ins = v.tensor_sub(xn_dst[:, i * TOWN + off:i * TOWN + off + 256],
                                       tbs[:], mbuf[:])
                ins.then_inc(sLNv, 1)
            v.wait_ge(sGar, 16)
            v.tensor_scalar(row6[0:1, 3:6], arsb[0:1, 0:3], 1.0 / NW, EPS_Q,
                            ALU.mult, ALU.add).then_inc(sMisc, 1)
            _mv[0] += 1
            v.wait_ge(sMisc, _mv[0])
            v.reciprocal(row6[0:1, 0:3], row6[0:1, 3:6]).then_inc(sMisc, 1)
            _mv[0] += 1
            v.wait_ge(sMisc, _mv[0])
            v.tensor_mul(row6[0:1, 4:5], row6[0:1, 4:5],
                         row6[0:1, 5:6]).then_inc(sVq2, 1)
            v.wait_ge(sTb, 1)
            v.tensor_copy(colbuf[:], pb[6][:, 0:6]).then_inc(sMisc, 1)
            _mv[0] += 1
            v.wait_ge(sMisc, _mv[0])
            v.tensor_scalar_mul(ncolbuf[:], colbuf[:, 0:3], -1.0).then_inc(sVq2, 1)
            for t in range(3 * NQT):
                m, i = divmod(t, NQT)
                v.wait_ge(sQr[t % 2], 16 * (3 * NQT // 2 + t // 2 + 1))
                if t >= 2:
                    v.wait_ge(sStq, 16 * t)
                v.tensor_scalar(gq1[:], qraw[t % 2][:], colbuf[:, m:m + 1], 0.5,
                                ALU.mult, ALU.is_gt)
                v.tensor_scalar(gq2[:], qraw[t % 2][:], ncolbuf[:, m:m + 1], 0.5,
                                ALU.mult, ALU.is_gt).then_inc(sMisc, 1)
                _mv[0] += 1
                v.wait_ge(sMisc, _mv[0])
                v.tensor_sub(qtile[t % 2][:], gq1[:], gq2[:]).then_inc(sVq2, 1)
            v.tensor_copy(tbs[0:1, 0:2], qtile[1][0:1, 0:2]).then_inc(sVq2, 1)
            for g_ in range(NG):
                v.wait_ge(sA2t, 5 * g_ + 1)
                if g_ >= 1:
                    v.wait_ge(sA2s, 4 * (g_ - 1) + 2)
                v.tensor_scalar_max(laq[:, 0:64], pb[2][:, 0:64], -CLAMP
                                    ).then_inc(sA2v, 1)
                v.wait_ge(sA2s, 4 * g_ + 2)
                v.wait_ge(sA2t, 5 * g_ + 2)
                v.tensor_mul(qtk[:, 0:64], qbuf[g_][:], atok[:])
                v.tensor_sub(dfs[:], laq[:, 0:64], pb[3][:, 0:64]
                             ).then_inc(sA2v, 1)
                v.wait_ge(sA2s, 4 * g_ + 3)
                v.tensor_mul(t1b[:], sgd[g_][:], entok[:]).then_inc(sMisc, 1)
                _mv[0] += 1
                v.wait_ge(sMisc, _mv[0])
                v.tensor_mul(ktk[:, 0:64], t1b[:], zkb[g_][:])
                v.tensor_mul(t2b[:], sgd[g_][:], ectok[:]).then_inc(sMisc, 1)
                _mv[0] += 1
                v.wait_ge(sMisc, _mv[0])
                v.tensor_mul(t1b[:], t2b[:], zkb[g_][:]).then_inc(sMisc, 1)
                _mv[0] += 1
                v.wait_ge(sMisc, _mv[0])
                v.tensor_copy(khat[g_][:], t1b[:]).then_inc(sA2v, 1)
                v.wait_ge(sA2t, 5 * g_ + 5)
                if g_ >= 8:
                    v.tensor_copy(qtl[g_ - 8][:], pb[5][0:64, 0:128])
                v.tensor_copy(ktil[g_][:], pb[6][0:64, 0:128]).then_inc(sA2v, 1)
            v.wait_ge(sLd, 16 * 5)
            for g_ in range(NG):
                T0, V0 = Tst[g_], Vst[g_]
                if g_ < 8:
                    v.wait_ge(sA3t, T0 + 1)
                    v.scalar_tensor_tensor(Sst[:], Sst[:], acb[g_][:, 0:1],
                                           pb[3][0:64, 0:128], ALU.mult,
                                           ALU.add).then_inc(sMisc, 1)
                    _mv[0] += 1
                    v.wait_ge(sMisc, _mv[0])
                    v.tensor_copy(Sb[:], Sst[:]).then_inc(sA3v, 1)
                    v.wait_ge(sA3t, T0 + 2)
                    v.scalar_tensor_tensor(Sst[:], Sst[:], acb[g_][:, 1:2],
                                           pb[4][0:64, 0:128], ALU.mult,
                                           ALU.add).then_inc(sMisc, 1)
                    _mv[0] += 1
                    v.wait_ge(sMisc, _mv[0])
                    v.tensor_copy(Sb[:], Sst[:]).then_inc(sA3v, 1)
                    if g_ == 7:
                        v.tensor_scalar_mul(Sst[:], Sst[:],
                                            gate_s[:]).then_inc(sMisc, 1)
                        _mv[0] += 1
                        v.wait_ge(sMisc, _mv[0])
                        v.tensor_copy(Sb[:], Sst[:]).then_inc(sA3v, 1)
                else:
                    v.wait_ge(sA3t, T0 + 2)
                    v.tensor_mul(pmb[:], pb[2][:, 0:128], mask_s[:]
                                 ).then_inc(sA3v, 1)
                    v.wait_ge(sA3t, T0 + 5)
                    v.scalar_tensor_tensor(Sst[:], Sst[:], acb[g_][:, 0:1],
                                           pb[3][0:64, 0:128], ALU.mult,
                                           ALU.add).then_inc(sMisc, 1)
                    _mv[0] += 1
                    v.wait_ge(sMisc, _mv[0])
                    v.tensor_copy(Sb[:], Sst[:]).then_inc(sA3v, 1)
                    v.wait_ge(sA3t, T0 + 6)
                    v.scalar_tensor_tensor(Sst[:], Sst[:], acb[g_][:, 1:2],
                                           pb[4][0:64, 0:128], ALU.mult,
                                           ALU.add).then_inc(sMisc, 1)
                    _mv[0] += 1
                    v.wait_ge(sMisc, _mv[0])
                    v.tensor_copy(Sb[:], Sst[:]).then_inc(sA3v, 1)
                    v.tensor_copy(Ob[:, (g_ - 8) * 128:(g_ - 7) * 128],
                                  pb[5][:, 0:128]).then_inc(sA3v, 1)
            for it in range(32):
                v.wait_ge(sOPt, it + 1)
                v.wait_ge(sXop[it % 2], 16 * (it // 2 + 1))
                if it >= 2:
                    v.wait_ge(sStY, 16 * (it - 1))
                v.tensor_add(yst[it % 2][:], xo[it % 2][:], pb[6 + it % 2][:]
                             ).then_inc(sOPv, 1)
            v.tensor_copy(tbs[0:1, 0:2], yst[1][0:1, 0:2]).then_inc(sOPv, 1)
            for ec in range(NEC):
                bA = 0 if ec % 2 == 0 else 4
                if ec == GUARD_QT_EC:
                    v.wait_ge(sStq, 16 * 49)
                if ec == GUARD_YS_EC:
                    v.wait_ge(sStY, 16 * 32)
                v.wait_ge(sFUt, 2 * ec + 2)
                v.wait_ge(sFUs, ec + 1)
                v.tensor_mul(hbuf[:, ec * TOWN:ec * TOWN + 512],
                             sgf[:, 0:512], pb[bA + 2][:])
                v.tensor_mul(hbuf[:, ec * TOWN + 512:(ec + 1) * TOWN],
                             sgf[:, 512:1024], pb[bA + 3][:]).then_inc(sFUv, 1)
            for dc in range(NDC):
                bY = 0 if dc % 2 == 0 else 2
                if dc >= 2:
                    v.wait_ge(sSto, 16 * (dc - 1))
                v.wait_ge(sFDt, 2 * dc + 2)
                v.wait_ge(sYr[dc % 2], 16 * (dc // 2 + 1))
                v.scalar_tensor_tensor(ostage[dc % 2][:, 0:512], pb[bY][:],
                                       colbuf[:, 4:5], ybuf[dc % 2][:, 0:512],
                                       ALU.mult, ALU.add)
                v.scalar_tensor_tensor(ostage[dc % 2][:, 512:1024], pb[bY + 1][:],
                                       colbuf[:, 4:5], ybuf[dc % 2][:, 512:1024],
                                       ALU.mult, ALU.add).then_inc(sFDv, 1)
            v.tensor_copy(tbs[0:1, 0:2], ostage[1][0:1, 0:2]).then_inc(sFDv, 1)

        @blk.scalar
        def _(s):
            for sl in range(NSLAB):
                s.wait_ge(sGx, 16 * (sl + 1))
                if sl >= 1:
                    s.wait_ge(sLNt, 2 * sl - 1)
                for i in range(NDC):
                    ins = s.activation(x2q[:, i * 256:(i + 1) * 256],
                                       xq[:, i * 256:(i + 1) * 256], AF.Square)
                ins.then_inc(sLNs, 1)
                s.wait_ge(sLNv, 3 * sl + 1)
                s.activation(varr[:], varr[:], AF.Ln).then_inc(sMiscS, 1)
                _ms[0] += 1
                s.wait_ge(sMiscS, _ms[0])
                s.activation(rstdr[:], varr[:], AF.Exp, scale=-0.5
                             ).then_inc(sLNs, 1)
            for g_ in range(NG):
                s.wait_ge(sZt, g_ + 1)
                bz = pb[g_ % 2]
                s.activation(sgb[g_][:], bz[:, 0:64], AF.Sigmoid)
                s.activation(sgd[g_][:], bz[:, 64:128], AF.Sigmoid, scale=-1.0)
                s.copy(vbuf[g_][:], bz[:, 128:256])
                s.copy(qbuf[g_][:], bz[:, 256:320])
                s.copy(zkb[g_][:], bz[:, 320:384]).then_inc(sZs, 1)
            for g_ in range(NG):
                s.activation(sgb[g_][:], sgb[g_][:], AF.Ln)
                s.activation(sgd[g_][:], sgd[g_][:], AF.Ln).then_inc(sA2s, 1)
                s.wait_ge(sA2v, 4 * g_ + 1)
                s.activation(atok[:], laq[:, 0:64], AF.Exp)
                s.activation(entok[:], laq[:, 0:64], AF.Exp, scale=-1.0
                             ).then_inc(sA2s, 1)
                s.wait_ge(sA2v, 4 * g_ + 2)
                s.activation(ectok[:], dfs[:], AF.Exp, scale=-1.0).then_inc(sA2s, 1)
                s.wait_ge(sA2t, 5 * g_ + 3)
                s.activation(acb[g_][:, 0:1], pb[4][0:64, 63:64], AF.Exp)
                s.activation(acb[g_][:, 1:2], pb[4][0:64, 127:128], AF.Exp
                             ).then_inc(sA2s, 1)
            s.wait_ge(sVq2, 2)
            s.wait_ge(sCCg, 1)
            s.wait_ge(sCCb, 1)
            for ec in range(2):
                s.dma_start(out=bufG[ec][:],
                            in_=ag_out[1].ap()[:, ec * 2048:(ec + 1) * 2048]
                            ).then_inc(sFWg[ec], 16)
            for ec in range(NEC):
                bA = 0 if ec % 2 == 0 else 4
                s.wait_ge(sFUt, 2 * ec + 1)
                if ec >= 1:
                    s.wait_ge(sFUv, ec)
                s.activation(sgf[:, 0:512], pb[bA][:], AF.Silu,
                             scale=colbuf[:, 3:4])
                s.activation(sgf[:, 512:1024], pb[bA + 1][:], AF.Silu,
                             scale=colbuf[:, 3:4]).then_inc(sFUs, 1)
                if ec + 2 < NEC:
                    s.wait_ge(sFUt, 2 * ec + 2)
                    s.dma_start(out=bufG[ec % 2][:],
                                in_=ag_out[1].ap()[:,
                                    (ec + 2) * 2048:(ec + 3) * 2048]
                                ).then_inc(sFWg[ec % 2], 16)
            s.wait_ge(sCCd, 1)
            s.wait_ge(sFUt, 2 * NEC)
            for dc in range(NDC):
                base = dc * 8192
                if dc >= 1:
                    s.wait_ge(sFDt, 2 * (dc - 1) + 2)
                s.dma_start(out=bufD[1][:],
                            in_=ag_out[2].ap()[:, base + 4096:base + 8192]
                            ).then_inc(sFW2[1], 16)

        @blk.tensor
        def _(t):
            t.wait_ge(sLd, 16 * 5)
            t.wait_ge(sGc, 16 * 3)
            t.wait_ge(sVq1, 3 * NQT + 3)
            for m in range(3):
                t.matmul(pb[6][0:1, m:m + 1], scol[:, m:m + 1], onesc_f[:],
                         start=True, stop=True).then_inc(sTq, 1)
            for sl in range(NSLAB):
                if sl >= 1:
                    t.wait_ge(sLNv, 3 * (sl - 1) + 1)
                t.wait_ge(sLNs, 2 * sl + 1)
                for i in range(NDC):
                    t.matmul(pb[0][0:1, 0:256], onesc_b[:],
                             xq[:, i * 256:(i + 1) * 256],
                             start=(i == 0), stop=(i == NDC - 1))
                    ins = t.matmul(pb[1][0:1, 0:256], onesc_b[:],
                                   x2q[:, i * 256:(i + 1) * 256],
                                   start=(i == 0), stop=(i == NDC - 1))
                ins.then_inc(sLNt, 1)
                t.wait_ge(sLNv, 3 * sl + 2)
                if sl >= 1:
                    t.wait_ge(sLNv, 3 * (sl - 1) + 3)
                t.matmul(pb[2][:, 0:256], onesr_b[:], rstdb[:], start=True, stop=True)
                t.matmul(pb[3][:, 0:256], onesr_b[:], mrb[:], start=True, stop=True
                         ).then_inc(sLNt, 1)
            t.wait_ge(sVq2, 1)
            t.matmul(pb[6][:, 0:6], onesr_f[:], row6[:], start=True, stop=True
                     ).then_inc(sTb, 1)
            t.wait_ge(sZv, 1)
            t.wait_ge(sLNv, 24)
            for g_ in range(NG):
                xn_src = xn_pri if g_ < 8 else xn_own
                goff = (g_ % 8) * 128
                if g_ >= 2:
                    t.wait_ge(sZs, g_ - 1)
                bz = pb[g_ % 2]
                t.matmul(bz[:, 0:384], onesr_b[:], brow_b[:], start=True, stop=False)
                for i in range(NDC):
                    ins = t.matmul(bz[:, 0:384],
                                   xn_src[:, i * TOWN + goff:i * TOWN + goff + 128],
                                   wz_b[:, i * 384:(i + 1) * 384],
                                   start=False, stop=(i == NDC - 1))
                ins.then_inc(sZt, 1)
            t.wait_ge(sVq2, 2)
            for g_ in range(NG):
                t.wait_ge(sA2s, 4 * g_ + 1)
                t.matmul(pb[2][:, 0:64], tri_s[:], sgb[g_][:], start=True, stop=True
                         ).then_inc(sA2t, 1)
                t.wait_ge(sA2v, 4 * g_ + 1)
                t.matmul(pb[3][:, 0:64], gsel_s[:], laq[:, 0:64], start=True,
                         stop=True).then_inc(sA2t, 1)
                t.transpose(pb[4][:, 0:128], laq[:], iden_s[:]).then_inc(sA2t, 1)
                t.wait_ge(sA2v, 4 * g_ + 2)
                t.transpose(pb[5][:, 0:128], qtk[:], iden_s[:]).then_inc(sA2t, 1)
                t.wait_ge(sA2v, 4 * g_ + 3)
                t.transpose(pb[6][:, 0:128], ktk[:], iden_s[:]).then_inc(sA2t, 1)
            t.wait_ge(sA2v, 4 * NG)
            t.wait_ge(sA2s, 4 * NG)
            for g_ in range(NG):
                T0, V0 = Tst[g_], Vst[g_]
                t.wait_ge(sA3v, V0)
                t.matmul(pb[3][0:64, 0:128], khat[g_][0:64, :], vbuf[g_][0:64, :],
                         start=True, stop=True).then_inc(sA3t, 1)
                if g_ < 8:
                    t.matmul(pb[4][0:64, 0:128], khat[g_][64:128, :],
                             vbuf[g_][64:128, :], start=True, stop=True
                             ).then_inc(sA3t, 1)
                else:
                    t.matmul(pb[2][:, 0:128], ktil[g_][:], qtl[g_ - 8][:],
                             start=True, stop=True).then_inc(sA3t, 1)
                    t.matmul(pb[4][0:64, 0:128], khat[g_][64:128, :],
                             vbuf[g_][64:128, :], start=True, stop=True
                             ).then_inc(sA3t, 1)
                    t.wait_ge(sA3v, V0 + 1)
                    t.matmul(pb[5][:, 0:128], vbuf[g_][:], pmb[:],
                             start=True, stop=False).then_inc(sA3t, 1)
                    t.matmul(pb[5][:, 0:64], Sb[:], qtl[g_ - 8][:, 0:64],
                             start=False, stop=False).then_inc(sA3t, 1)
                    t.wait_ge(sA3v, V0 + 2)
                    t.matmul(pb[5][:, 64:128], Sb[:], qtl[g_ - 8][:, 64:128],
                             start=False, stop=True).then_inc(sA3t, 1)
            t.wait_ge(sA3v, A3V_TOT)
            for it in range(32):
                half, dc = divmod(it, NDC)
                if it >= 2:
                    t.wait_ge(sOPv, it)
                t.matmul(pb[6 + it % 2][:], wo_b[:, dc * 128:(dc + 1) * 128],
                         Ob[:, half * 512:(half + 1) * 512],
                         start=True, stop=True).then_inc(sOPt, 1)
            t.wait_ge(sCCu, 1)
            t.wait_ge(sCCg, 1)
            t.wait_ge(sOPv, 32)
            for ec in range(NEC):
                t.wait_ge(sFW[ec % 2], 16 * (ec // 2 + 1))
                if ec >= 2:
                    t.wait_ge(sFUv, ec - 1)
                    t.wait_ge(sFUs, ec - 1)
                bA = 0 if ec % 2 == 0 else 4
                for i in range(NDC):
                    lu = bufU[ec % 2][:, i * 128:(i + 1) * 128]
                    t.matmul(pb[bA][:], lu, xn_own[:, i * TOWN:i * TOWN + 512],
                             start=(i == 0), stop=(i == NDC - 1))
                    ins = t.matmul(pb[bA + 1][:], lu,
                                   xn_own[:, i * TOWN + 512:(i + 1) * TOWN],
                                   start=(i == 0), stop=(i == NDC - 1))
                ins.then_inc(sFUt, 1)
                t.wait_ge(sFWg[ec % 2], 16 * (ec // 2 + 1))
                for i in range(NDC):
                    lg = bufG[ec % 2][:, i * 128:(i + 1) * 128]
                    t.matmul(pb[bA + 2][:], lg, xn_own[:, i * TOWN:i * TOWN + 512],
                             start=(i == 0), stop=(i == NDC - 1))
                    ins = t.matmul(pb[bA + 3][:], lg,
                                   xn_own[:, i * TOWN + 512:(i + 1) * TOWN],
                                   start=(i == 0), stop=(i == NDC - 1))
                ins.then_inc(sFUt, 1)
            t.wait_ge(sCCd, 1)
            t.wait_ge(sFUv, NEC)
            for dc in range(NDC):
                bY = 0 if dc % 2 == 0 else 2
                if dc >= 1:
                    t.wait_ge(sFDv, dc)
                t.wait_ge(sFW2[0], 16 * (dc + 1))
                for ec in range(32):
                    ld = bufD[0][:, ec * 128:(ec + 1) * 128]
                    t.matmul(pb[bY][:], ld, hbuf[:, ec * TOWN:ec * TOWN + 512],
                             start=(ec == 0), stop=False)
                    ins = t.matmul(pb[bY + 1][:], ld,
                                   hbuf[:, ec * TOWN + 512:(ec + 1) * TOWN],
                                   start=(ec == 0), stop=False)
                ins.then_inc(sFDt, 1)
                t.wait_ge(sFW2[1], 16 * (dc + 1))
                for ec in range(32, NEC):
                    ld = bufD[1][:, (ec - 32) * 128:(ec - 31) * 128]
                    t.matmul(pb[bY][:], ld, hbuf[:, ec * TOWN:ec * TOWN + 512],
                             start=False, stop=(ec == NEC - 1))
                    ins = t.matmul(pb[bY + 1][:], ld,
                                   hbuf[:, ec * TOWN + 512:(ec + 1) * TOWN],
                                   start=False, stop=(ec == NEC - 1))
                ins.then_inc(sFDt, 1)

    return nc


def _pretile_updown(W, kind):
    if kind == "up":
        P = W.reshape(64, 128, 16, 128).transpose(3, 0, 2, 1)
    else:
        P = W.reshape(16, 128, 64, 128).transpose(3, 0, 2, 1)
    return np.ascontiguousarray(P.reshape(128, 131072))


def _consts():
    tri = np.zeros((128, 128), np.float32)
    gsel = np.zeros((128, 128), np.float32)
    mask = np.zeros((128, 128), np.float32)
    for u in range(128):
        for t in range(128):
            if u // 64 == t // 64:
                if u <= t:
                    tri[u, t] = 1.0
                    mask[u, t] = 1.0
        gsel[(u // 64) * 64 + 63, u] = 1.0
    iden = np.eye(128, dtype=np.float32)
    return tri, gsel, mask, iden


_CACHE = {}


def kernel(L0_attn_x, L0_attn_Wq, L0_attn_Wk, L0_attn_Wv, L0_attn_Wo,
           L0_attn_Wb, L0_attn_bb, L0_attn_Wd, L0_attn_bd,
           L0_ffn_W_up, L0_ffn_W_gate, L0_ffn_W_down):
    x = np.asarray(L0_attn_x, np.float32)
    tri, gsel, mask, iden = _consts()
    wzc = np.concatenate([np.asarray(L0_attn_Wb, np.float32),
                          np.asarray(L0_attn_Wd, np.float32),
                          np.asarray(L0_attn_Wv, np.float32),
                          np.asarray(L0_attn_Wq, np.float32),
                          np.asarray(L0_attn_Wk, np.float32)], axis=1)
    wzt = np.ascontiguousarray(
        wzc.reshape(16, 128, 384).transpose(1, 0, 2).reshape(128, 6144))
    browc = np.zeros((1, 384), np.float32)
    browc[0, 0:64] = np.asarray(L0_attn_bb, np.float32)
    browc[0, 64:128] = np.asarray(L0_attn_bd, np.float32)
    wo = np.ascontiguousarray(np.asarray(L0_attn_Wo, np.float32))

    Pu = _pretile_updown(np.asarray(L0_ffn_W_up, np.float32), "up")
    Pg = _pretile_updown(np.asarray(L0_ffn_W_gate, np.float32), "up")
    Pd = _pretile_updown(np.asarray(L0_ffn_W_down, np.float32), "down")

    if "nc" not in _CACHE:
        _CACHE["nc"] = build_program(debug=False)
    nc = _CACHE["nc"]

    in_maps = []
    for c in range(N_CORES):
        b, h = divmod(c, 2)
        xT = np.ascontiguousarray(x[b].T)
        r0, r1 = 16 * c, 16 * (c + 1)

        def slabtile(a):
            return np.ascontiguousarray(
                a.reshape(16, 128, 4, 256).transpose(1, 2, 0, 3).reshape(128, 16384))
        xow = xT[:, h * 1024:(h + 1) * 1024]
        in_maps.append(dict(
            x_own=np.ascontiguousarray(xow),
            xt_own=slabtile(xow),
            xt_pri=slabtile(xT[:, 0:1024]),
            gatec=np.full((64, 1), float(h), np.float32),
            wz=wzt, brow=browc, wo=wo,
            tri=tri, gsel=gsel, maskc=mask, iden=iden,
            raw_u=np.ascontiguousarray(Pu[r0:r1].reshape(128, 16384)),
            raw_g=np.ascontiguousarray(Pg[r0:r1].reshape(128, 16384)),
            raw_d=np.ascontiguousarray(Pd[r0:r1].reshape(128, 16384)),
        ))

    res = run_bass_kernel_spmd(nc, in_maps, core_ids=list(range(N_CORES)))

    y = np.empty((4, 2048, 2048), np.float32)
    for c in range(N_CORES):
        b, h = divmod(c, 2)
        y[b, h * 1024:(h + 1) * 1024, :] = res.results[c]["out"].T
    return y


# version 18 (final)
# speedup vs baseline: 1.1347x; 1.1347x over previous
import contextlib

import numpy as np

import concourse.bass as bass
import concourse.mybir as mybir
from concourse.bass_utils import run_bass_kernel_spmd

F32 = mybir.dt.float32
BF16 = mybir.dt.bfloat16
AF = mybir.ActivationFunctionType
ALU = mybir.AluOpType
AX = mybir.AxisListType.X

N_CORES = 8
D = 2048
TOWN = 1024
E = 8192
NDC = D // 128
NEC = E // 128
NG = 16
NSLAB = 8
NQT = 16
CLAMP = 75.0
EPS_LN = 1e-5
EPS_Q = 1e-8
NW = float(D * E)


def _roundup(x, a=32):
    return (x + a - 1) // a * a


def build_program(debug=False):
    nc = bass.Bass("TRN2", target_bir_lowering=False,
                   detect_race_conditions=False)

    x_own = nc.dram_tensor("x_own", [D, TOWN], F32, kind="ExternalInput")
    xt_own = nc.dram_tensor("xt_own", [128, 16384], F32, kind="ExternalInput")
    xt_pri = nc.dram_tensor("xt_pri", [128, 16384], F32, kind="ExternalInput")
    gatec = nc.dram_tensor("gatec", [64, 1], F32, kind="ExternalInput")
    wz = nc.dram_tensor("wz", [128, NDC * 384], F32, kind="ExternalInput")
    brow = nc.dram_tensor("brow", [1, 384], F32, kind="ExternalInput")
    wo = nc.dram_tensor("wo", [128, D], F32, kind="ExternalInput")
    tri = nc.dram_tensor("tri", [128, 128], F32, kind="ExternalInput")
    gsel = nc.dram_tensor("gsel", [128, 128], F32, kind="ExternalInput")
    maskc = nc.dram_tensor("maskc", [128, 128], F32, kind="ExternalInput")
    iden = nc.dram_tensor("iden", [128, 128], F32, kind="ExternalInput")
    raw_u = nc.dram_tensor("raw_u", [128, 16384], F32, kind="ExternalInput")
    raw_g = nc.dram_tensor("raw_g", [128, 16384], F32, kind="ExternalInput")
    raw_d = nc.dram_tensor("raw_d", [128, 16384], F32, kind="ExternalInput")
    out = nc.dram_tensor("out", [D, TOWN], F32, kind="ExternalOutput")
    if debug:
        dbg_xn = nc.dram_tensor("dbg_xn", [128, TOWN], F32, kind="ExternalOutput")
        dbg_row6 = nc.dram_tensor("dbg_row6", [1, 6], F32, kind="ExternalOutput")
        dbg_yp = nc.dram_tensor("dbg_yp", [D, TOWN], F32, kind="ExternalOutput")
        dbg_S = nc.dram_tensor("dbg_S", [64, 128], F32, kind="ExternalOutput")
        dbg_h = nc.dram_tensor("dbg_h", [128, TOWN], F32, kind="ExternalOutput")
        dbg_w = nc.dram_tensor("dbg_w", [128, 4096], F32, kind="ExternalOutput")

    ar_in = nc.dram_tensor("ar_in", [1, 4], F32)
    ar_out = nc.dram_tensor("ar_out", [1, 4], F32)
    ag_in = [nc.dram_tensor(f"ag_in{m}", [128, 16384], BF16) for m in range(3)]
    ag_out = [nc.dram_tensor(f"ag_out{m}", [128, 131072], BF16) for m in range(3)]
    y_sc = nc.dram_tensor("y_sc", [D, TOWN], F32)

    RG = [list(range(N_CORES))]

    ARENA_BYTES = 209920
    SB_TOP = ARENA_BYTES
    arena = nc.alloc_sbuf_tensor("arena", [128, ARENA_BYTES], mybir.dt.uint8)
    _off = [0]
    _offmap = {}

    def alloc(name, shape, dt, at=None):
        if at is not None:
            _off[0] = at
        per = 1
        for s in shape[1:]:
            per *= s
        per *= mybir.dt.size(dt)
        ap = arena.ap()[0:shape[0], _off[0]:_off[0] + per].bitcast(dt)
        _offmap[name] = _off[0]
        _off[0] += _roundup(per)
        return ap

    tri_s = alloc("tri_s", [128, 128], F32)
    gsel_s = alloc("gsel_s", [128, 128], F32)
    mask_s = alloc("mask_s", [128, 128], F32)
    iden_s = alloc("iden_s", [128, 128], F32)
    gate_s = alloc("gate_s", [64, 1], F32)
    wo_b = alloc("wo_b", [128, D], BF16)
    brow_b = alloc("brow_b", [1, 384], BF16)
    onesr_b = alloc("onesr_b", [1, 128], BF16)
    onesc_b = alloc("onesc_b", [128, 1], BF16)
    onesr_f = alloc("onesr_f", [1, 128], F32)
    onesc_f = alloc("onesc_f", [128, 1], F32)
    colbuf = alloc("colbuf", [128, 6], F32)
    ncolbuf = alloc("ncolbuf", [128, 3], F32)
    row6 = alloc("row6", [1, 6], F32)
    arbuf = alloc("arbuf", [1, 4], F32)
    arsb = alloc("arsb", [1, 4], F32)
    Sst = alloc("Sst", [64, 128], F32)
    Sb = alloc("Sb", [64, 128], BF16)
    xn_own = alloc("xn_own", [128, NDC * TOWN], BF16)
    PERS_END = _off[0]

    A0 = PERS_END
    xn_pri = alloc("xn_pri", [128, NDC * TOWN], BF16, at=A0)
    xq = alloc("xq", [128, NDC * 256], BF16)
    x2t = [alloc(f"x2t{i}", [128, 256], BF16) for i in range(2)]
    wz_b = alloc("wz_b", [128, NDC * 384], BF16)
    qraw = [alloc(f"qraw{i}", [128, 1024], F32) for i in range(4)]
    qtile = [alloc(f"qtile{i}", [128, 1024], BF16) for i in range(2)]
    gq1 = alloc("gq1", [128, 1024], BF16)
    gq2 = alloc("gq2", [128, 1024], BF16)
    pcols = alloc("pcols", [128, 48], F32)
    scol = alloc("scol", [128, 3], F32)
    mrow = alloc("mrow", [1, 256], F32)
    qrow = alloc("qrow", [1, 256], F32)
    msq = alloc("msq", [1, 256], F32)
    varr = alloc("varr", [1, 256], F32)
    rstdr = alloc("rstdr", [1, 256], F32)
    mrr = alloc("mrr", [1, 256], F32)
    rstdb = alloc("rstdb", [1, 256], BF16)
    mrb = alloc("mrb", [1, 256], BF16)
    rbuf = alloc("rbuf", [128, 256], BF16)
    mbuf = alloc("mbuf", [128, 256], BF16)
    tbs = alloc("tbs", [128, 256], BF16)
    sgb = [alloc(f"sgb{g}", [128, 64], F32) for g in range(NG)]
    sgd = [alloc(f"sgd{g}", [128, 64], F32) for g in range(NG)]
    vbuf = [alloc(f"vbuf{g}", [128, 128], BF16) for g in range(NG)]
    zkb = [alloc(f"zkb{g}", [128, 64], F32) for g in range(NG)]
    qbuf = [alloc(f"qbuf{g}", [128, 64], F32) for g in range(NG)]
    acb = [alloc(f"acb{g}", [64, 2], F32) for g in range(NG)]
    khat = [alloc(f"khat{g}", [128, 64], BF16) for g in range(NG)]
    ktil = [alloc(f"ktil{g}", [64, 128], BF16) for g in range(NG)]
    qtl = [alloc(f"qtl{g}", [64, 128], BF16) for g in range(8)]
    laq = alloc("laq", [128, 128], F32)
    qtk = alloc("qtk", [128, 128], F32)
    ktk = alloc("ktk", [128, 128], F32)
    atok = alloc("atok", [128, 64], F32)
    entok = alloc("entok", [128, 64], F32)
    ectok = alloc("ectok", [128, 64], F32)
    dfs = alloc("dfs", [128, 64], F32)
    t1b = alloc("t1b", [128, 64], F32)
    t2b = alloc("t2b", [128, 64], F32)
    pmb = alloc("pmb", [128, 128], BF16)
    Ob = alloc("Ob", [128, TOWN], BF16)
    xo = [alloc(f"xo{i}", [128, 512], F32) for i in range(2)]
    yst = [alloc(f"yst{i}", [128, 512], F32) for i in range(2)]
    A_END = _off[0]

    hbuf = alloc("hbuf", [128, NEC * TOWN], BF16, at=A0)
    B1 = _off[0]
    bufU = [alloc(f"bufU{i}", [128, 2048], BF16) for i in range(2)]
    bufG = [alloc(f"bufG{i}", [128, 2048], BF16) for i in range(2)]
    B2 = _off[0]
    bufD = [alloc(f"bufD{i}", [128, 4096], BF16, at=B1 if i == 0 else None)
            for i in range(2)]
    _off[0] = B2
    sgf = alloc("sgf", [128, 1024], F32, at=B2)
    ybuf = [alloc(f"ybuf{i}", [128, 1024], F32) for i in range(2)]
    ostage = [alloc(f"ostage{i}", [128, 1024], F32) for i in range(2)]
    B_END = _off[0]
    assert A_END <= SB_TOP and B_END <= SB_TOP, (PERS_END, A_END, B_END)
    assert B1 >= A_END, (B1, A_END)

    GUARD_QT_EC = max(0, (_offmap["qraw0"] - A0) // 2048 - 1)
    GUARD_YS_EC = max(0, (_offmap["Ob"] - A0) // 2048 - 1)

    es = contextlib.ExitStack()
    with es:
        def sem(name):
            return es.enter_context(nc.semaphore(name))

        pb = [es.enter_context(nc.psum_tensor(f"pb{i}", [128, 512], F32))
              for i in range(8)]

        sLd = sem("sLd"); sGc = sem("sGc")
        sQr = [sem("sQrA"), sem("sQrB"), sem("sQrC"), sem("sQrD")]
        sVq1 = sem("sVq1"); sVq2 = sem("sVq2")
        sTq = sem("sTq"); sTb = sem("sTb")
        sCCar = sem("sCCar"); sCCu = sem("sCCu"); sCCg = sem("sCCg")
        sCCd = sem("sCCd")
        sGar = sem("sGar"); sStq = sem("sStq")
        sGx = sem("sGx"); sLNs = sem("sLNs"); sLNt = sem("sLNt"); sLNv = sem("sLNv")
        sZt = sem("sZt"); sZv = sem("sZv"); sZs = sem("sZs")
        sA2t = sem("sA2t"); sA2v = sem("sA2v"); sA2s = sem("sA2s")
        sA3t = sem("sA3t"); sA3v = sem("sA3v")
        sXop = [sem("sXopA"), sem("sXopB")]
        sOPt = sem("sOPt"); sOPv = sem("sOPv"); sStY = sem("sStY")
        sFW = [sem("sFWA"), sem("sFWB")]
        sFUt = sem("sFUt"); sFUs = sem("sFUs"); sFUv = sem("sFUv")
        sFW2 = [sem("sFW2A"), sem("sFW2B")]
        sYr = [sem("sYrA"), sem("sYrB")]
        sFDt = sem("sFDt"); sFDv = sem("sFDv")
        sSto = sem("sSto"); sDbg = sem("sDbg"); sMisc = sem("sMisc")
        sMiscS = sem("sMiscS")
        sCCb = sem("sCCb")
        _mv = [0]
        _ms = [0]

        t_incs = [2 if g < 8 else 6 for g in range(NG)]
        v_incs = [2 if g < 8 else 4 for g in range(NG)]
        v_incs[7] += 1
        Tst = [0]; Vst = [0]
        for g in range(NG):
            Tst.append(Tst[-1] + t_incs[g])
            Vst.append(Vst[-1] + v_incs[g])
        A3T_TOT, A3V_TOT = Tst[NG], Vst[NG]

        raws = [raw_u, raw_g, raw_d]
        blk = es.enter_context(nc.Block())

        @blk.sync
        def _(sync):
            sync.dma_start(out=tri_s[:], in_=tri.ap()).then_inc(sLd, 16)
            sync.dma_start(out=gsel_s[:], in_=gsel.ap()).then_inc(sLd, 16)
            sync.dma_start(out=mask_s[:], in_=maskc.ap()).then_inc(sLd, 16)
            sync.dma_start(out=iden_s[:], in_=iden.ap()).then_inc(sLd, 16)
            sync.dma_start(out=gate_s[:], in_=gatec.ap()).then_inc(sLd, 16)
            for t in range(0, 3 * NQT, 2):
                m, i = divmod(t, NQT)
                if t >= 4:
                    sync.wait_ge(sVq1, t - 3)
                sync.dma_start(out=qraw[t % 4][:],
                               in_=raws[m].ap()[:, i * 1024:(i + 1) * 1024]
                               ).then_inc(sQr[t % 4], 16)
            sync.wait_ge(sVq1, 3 * NQT)
            for t in range(0, 3 * NQT, 2):
                m, i = divmod(t, NQT)
                if t >= 4:
                    sync.wait_ge(sVq2, t - 1)
                sync.dma_start(out=qraw[t % 4][:],
                               in_=raws[m].ap()[:, i * 1024:(i + 1) * 1024]
                               ).then_inc(sQr[t % 4], 16)
            for it in range(32):
                half, dc = divmod(it, NDC)
                if it >= 2:
                    sync.wait_ge(sOPv, it - 1)
                sync.dma_start(out=xo[it % 2][:],
                               in_=x_own.ap()[dc * 128:(dc + 1) * 128,
                                              half * 512:(half + 1) * 512]
                               ).then_inc(sXop[it % 2], 16)
            sync.wait_ge(sCCu, 1)
            sync.wait_ge(sCCg, 1)
            sync.wait_ge(sCCb, 1)
            for ec in range(NEC):
                if ec >= 2:
                    sync.wait_ge(sFUt, 2 * (ec - 2) + 2)
                sync.dma_start(out=bufU[ec % 2][:],
                               in_=ag_out[0].ap()[:, ec * 2048:(ec + 1) * 2048]
                               ).then_inc(sFW[ec % 2], 16)
                sync.dma_start(out=bufG[ec % 2][:],
                               in_=ag_out[1].ap()[:, ec * 2048:(ec + 1) * 2048]
                               ).then_inc(sFW[ec % 2], 16)
            sync.wait_ge(sCCd, 1)
            sync.wait_ge(sFUt, 2 * NEC)
            sync.wait_ge(sFUs, NEC)
            sync.wait_ge(sStY, 16 * 32)
            for dc in range(NDC):
                base = dc * 8192
                if dc >= 1:
                    sync.wait_ge(sFDt, 2 * (dc - 1) + 1)
                sync.dma_start(out=bufD[0][:],
                               in_=ag_out[2].ap()[:, base:base + 4096]
                               ).then_inc(sFW2[0], 16)
                if dc >= 1:
                    sync.wait_ge(sFDt, 2 * (dc - 1) + 2)
                sync.dma_start(out=bufD[1][:],
                               in_=ag_out[2].ap()[:, base + 4096:base + 8192]
                               ).then_inc(sFW2[1], 16)
                if dc >= 2:
                    sync.wait_ge(sFDv, dc - 1)
                sync.dma_start(out=ybuf[dc % 2][:],
                               in_=y_sc.ap()[dc * 128:(dc + 1) * 128, :]
                               ).then_inc(sYr[dc % 2], 16)

        @blk.gpsimd
        def _(g):
            g.dma_start(out=wz_b[:], in_=wz.ap()).then_inc(sGc, 16)
            g.dma_start(out=brow_b[:], in_=brow.ap()).then_inc(sGc, 16)
            g.dma_start(out=wo_b[:], in_=wo.ap()).then_inc(sGc, 16)
            for s in range(NSLAB):
                srcx = xt_pri if s < 4 else xt_own
                c0 = (s % 4) * 4096
                if s >= 1:
                    g.wait_ge(sLNv, 3 * s)
                g.dma_start(out=xq[:],
                            in_=srcx.ap()[:, c0:c0 + 4096]).then_inc(sGx, 16)
            g.wait_ge(sVq1, 3 * NQT + 5)
            g.dma_start(out=ar_in.ap(), in_=arbuf[:]).then_inc(sStq, 16)
            g.wait_ge(sStq, 16)
            g.collective_compute("AllReduce", ALU.add, replica_groups=RG,
                                 ins=[ar_in.ap()], outs=[ar_out.ap()]
                                 ).then_inc(sCCar, 1)
            g.wait_ge(sCCar, 1)
            g.dma_start(out=arsb[:], in_=ar_out.ap()).then_inc(sGar, 16)
            nst = 1
            for t in range(3 * NQT):
                m, i = divmod(t, NQT)
                g.wait_ge(sVq2, min(t + 4, 51))
                g.dma_start(out=ag_in[m].ap()[:, i * 1024:(i + 1) * 1024],
                            in_=qtile[t % 2][:]).then_inc(sStq, 16)
                nst += 1
                if i == NQT - 1:
                    g.wait_ge(sStq, 16 * nst)
                    g.collective_compute("AllGather", ALU.bypass, replica_groups=RG,
                                         ins=[ag_in[m].ap()], outs=[ag_out[m].ap()]
                                         ).then_inc([sCCu, sCCg, sCCd][m], 1)
            g.collective_compute("AllReduce", ALU.add, replica_groups=RG,
                                 ins=[ar_in.ap()], outs=[ar_out.ap()]
                                 ).then_inc(sCCb, 1)
            for it in range(32):
                half, dc = divmod(it, NDC)
                g.wait_ge(sOPv, min(it + 2, 33))
                g.dma_start(out=y_sc.ap()[dc * 128:(dc + 1) * 128,
                                          half * 512:(half + 1) * 512],
                            in_=yst[it % 2][:]).then_inc(sStY, 16)
            if debug:
                g.wait_ge(sStY, 16 * 32)
                g.dma_start(out=dbg_yp.ap(), in_=y_sc.ap()).then_inc(sDbg, 16)
                g.dma_start(out=dbg_xn.ap(), in_=xn_own[:, 0:TOWN]).then_inc(sDbg, 16)
                g.dma_start(out=dbg_row6.ap(), in_=row6[:]).then_inc(sDbg, 16)
                g.dma_start(out=dbg_S.ap(), in_=Sst[:]).then_inc(sDbg, 16)
                g.wait_ge(sDbg, 16 * 4)
            for dc in range(NDC):
                g.wait_ge(sFDv, min(dc + 2, 17))
                g.dma_start(out=out.ap()[dc * 128:(dc + 1) * 128, :],
                            in_=ostage[dc % 2][:]).then_inc(sSto, 16)
            if debug:
                g.dma_start(out=dbg_h.ap(), in_=hbuf[:, 0:TOWN]).then_inc(sDbg, 16)
                g.dma_start(out=dbg_w.ap(), in_=ag_out[0].ap()[:, 0:4096]
                            ).then_inc(sDbg, 16)
                g.wait_ge(sDbg, 16 * 6)
            g.wait_ge(sSto, 16 * NDC)

        @blk.vector
        def _(v):
            v.memset(onesr_b[:], 1.0)
            v.memset(onesc_b[:], 1.0)
            v.memset(onesr_f[:], 1.0)
            v.memset(onesc_f[:], 1.0)
            v.memset(Sst[:], 0.0)
            v.memset(Sb[:], 0.0)
            v.memset(laq[:], 0.0)
            v.memset(qtk[:], 0.0)
            v.memset(ktk[:], 0.0)
            v.wait_ge(sGc, 48)
            for i in range(NDC):
                ins = v.tensor_scalar_mul(wz_b[:, i * 384 + 320:i * 384 + 384],
                                          wz_b[:, i * 384 + 320:i * 384 + 384], -1.0)
            ins.then_inc(sZv, 1)
            for t in range(3 * NQT):
                v.wait_ge(sQr[t % 4], 16 * (t // 4 + 1))
                v.reduce_sum(pcols[:, t:t + 1], qraw[t % 4][:], AX,
                             apply_absolute_value=True).then_inc(sVq1, 1)
            v.wait_ge(sVq1, 3 * NQT)
            for m in range(3):
                v.reduce_sum(scol[:, m:m + 1], pcols[:, m * NQT:(m + 1) * NQT], AX
                             ).then_inc(sVq1, 1)
            v.wait_ge(sTq, 3)
            v.tensor_copy(arbuf[0:1, 0:3], pb[6][0:1, 0:3])
            v.memset(arbuf[0:1, 3:4], 0.0).then_inc(sVq1, 1)
            v.tensor_copy(tbs[0:1, 0:2], arbuf[0:1, 0:2]).then_inc(sVq1, 1)
            for s in range(NSLAB):
                xn_dst = xn_pri if s < 4 else xn_own
                off = (s % 4) * 256
                v.wait_ge(sLNt, 17 * s + 16)
                v.tensor_scalar_mul(mrow[:], pb[0][0:1, 0:256], 1.0 / D)
                v.tensor_scalar_mul(qrow[:], pb[1][0:1, 0:256],
                                    1.0 / D).then_inc(sMisc, 1)
                _mv[0] += 1
                v.wait_ge(sMisc, _mv[0])
                v.tensor_mul(msq[:], mrow[:], mrow[:]).then_inc(sMisc, 1)
                _mv[0] += 1
                v.wait_ge(sMisc, _mv[0])
                v.scalar_tensor_tensor(varr[:], qrow[:], EPS_LN, msq[:],
                                       ALU.add, ALU.subtract).then_inc(sLNv, 1)
                v.wait_ge(sLNs, 17 * s + 17)
                v.tensor_mul(mrr[:], mrow[:], rstdr[:]).then_inc(sMisc, 1)
                _mv[0] += 1
                v.wait_ge(sMisc, _mv[0])
                v.tensor_copy(rstdb[:], rstdr[:])
                v.tensor_copy(mrb[:], mrr[:]).then_inc(sLNv, 1)
                v.wait_ge(sLNt, 17 * s + 17)
                v.tensor_copy(rbuf[:], pb[2][:, 0:256])
                v.tensor_copy(mbuf[:], pb[3][:, 0:256]).then_inc(sMisc, 1)
                _mv[0] += 1
                v.wait_ge(sMisc, _mv[0])
                for i in range(NDC):
                    v.tensor_mul(tbs[:], xq[:, i * 256:(i + 1) * 256],
                                 rbuf[:]).then_inc(sMisc, 1)
                    _mv[0] += 1
                    v.wait_ge(sMisc, _mv[0])
                    ins = v.tensor_sub(xn_dst[:, i * TOWN + off:i * TOWN + off + 256],
                                       tbs[:], mbuf[:])
                ins.then_inc(sLNv, 1)
            v.wait_ge(sGar, 16)
            v.tensor_scalar(row6[0:1, 3:6], arsb[0:1, 0:3], 1.0 / NW, EPS_Q,
                            ALU.mult, ALU.add).then_inc(sMisc, 1)
            _mv[0] += 1
            v.wait_ge(sMisc, _mv[0])
            v.reciprocal(row6[0:1, 0:3], row6[0:1, 3:6]).then_inc(sMisc, 1)
            _mv[0] += 1
            v.wait_ge(sMisc, _mv[0])
            v.tensor_mul(row6[0:1, 4:5], row6[0:1, 4:5],
                         row6[0:1, 5:6]).then_inc(sVq2, 1)
            v.wait_ge(sTb, 1)
            v.tensor_copy(colbuf[:], pb[6][:, 0:6]).then_inc(sMisc, 1)
            _mv[0] += 1
            v.wait_ge(sMisc, _mv[0])
            v.tensor_scalar_mul(ncolbuf[:], colbuf[:, 0:3], -1.0).then_inc(sVq2, 1)
            for t in range(3 * NQT):
                m, i = divmod(t, NQT)
                v.wait_ge(sQr[t % 4], 16 * (t // 4 + 13))
                if t >= 2:
                    v.wait_ge(sStq, 16 * t)
                v.tensor_scalar(gq1[:], qraw[t % 4][:], colbuf[:, m:m + 1], 0.5,
                                ALU.mult, ALU.is_gt)
                v.tensor_scalar(gq2[:], qraw[t % 4][:], ncolbuf[:, m:m + 1], 0.5,
                                ALU.mult, ALU.is_gt).then_inc(sMisc, 1)
                _mv[0] += 1
                v.wait_ge(sMisc, _mv[0])
                v.tensor_sub(qtile[t % 2][:], gq1[:], gq2[:]).then_inc(sVq2, 1)
            v.tensor_copy(tbs[0:1, 0:2], qtile[1][0:1, 0:2]).then_inc(sVq2, 1)
            for g_ in range(NG):
                bz = pb[g_ % 2]
                v.wait_ge(sZt, g_ + 1)
                v.wait_ge(sZs, g_ + 1)
                v.tensor_copy(vbuf[g_][:], bz[:, 128:256])
                v.tensor_copy(qbuf[g_][:], bz[:, 256:320])
                v.tensor_copy(zkb[g_][:], bz[:, 320:384]).then_inc(sZv, 1)
            for g_ in range(NG):
                v.wait_ge(sA2t, 5 * g_ + 1)
                if g_ >= 1:
                    v.wait_ge(sA2s, 4 * (g_ - 1) + 2)
                v.tensor_scalar_max(laq[:, 0:64], pb[2][:, 0:64], -CLAMP
                                    ).then_inc(sA2v, 1)
                v.wait_ge(sA2s, 4 * g_ + 2)
                v.wait_ge(sA2t, 5 * g_ + 2)
                v.tensor_mul(qtk[:, 0:64], qbuf[g_][:], atok[:])
                v.tensor_sub(dfs[:], laq[:, 0:64], pb[3][:, 0:64]
                             ).then_inc(sA2v, 1)
                v.wait_ge(sA2s, 4 * g_ + 3)
                v.tensor_mul(t1b[:], sgd[g_][:], entok[:]).then_inc(sMisc, 1)
                _mv[0] += 1
                v.wait_ge(sMisc, _mv[0])
                v.tensor_mul(ktk[:, 0:64], t1b[:], zkb[g_][:])
                v.tensor_mul(t2b[:], sgd[g_][:], ectok[:]).then_inc(sMisc, 1)
                _mv[0] += 1
                v.wait_ge(sMisc, _mv[0])
                v.tensor_mul(t1b[:], t2b[:], zkb[g_][:]).then_inc(sMisc, 1)
                _mv[0] += 1
                v.wait_ge(sMisc, _mv[0])
                v.tensor_copy(khat[g_][:], t1b[:]).then_inc(sA2v, 1)
                v.wait_ge(sA2t, 5 * g_ + 5)
                if g_ >= 8:
                    v.tensor_copy(qtl[g_ - 8][:], pb[5][0:64, 0:128])
                v.tensor_copy(ktil[g_][:], pb[6][0:64, 0:128]).then_inc(sA2v, 1)
            v.wait_ge(sLd, 16 * 5)
            for g_ in range(NG):
                T0, V0 = Tst[g_], Vst[g_]
                if g_ < 8:
                    v.wait_ge(sA3t, T0 + 1)
                    v.scalar_tensor_tensor(Sst[:], Sst[:], acb[g_][:, 0:1],
                                           pb[3][0:64, 0:128], ALU.mult,
                                           ALU.add).then_inc(sMisc, 1)
                    _mv[0] += 1
                    v.wait_ge(sMisc, _mv[0])
                    v.tensor_copy(Sb[:], Sst[:]).then_inc(sA3v, 1)
                    v.wait_ge(sA3t, T0 + 2)
                    v.scalar_tensor_tensor(Sst[:], Sst[:], acb[g_][:, 1:2],
                                           pb[4][0:64, 0:128], ALU.mult,
                                           ALU.add).then_inc(sMisc, 1)
                    _mv[0] += 1
                    v.wait_ge(sMisc, _mv[0])
                    v.tensor_copy(Sb[:], Sst[:]).then_inc(sA3v, 1)
                    if g_ == 7:
                        v.tensor_scalar_mul(Sst[:], Sst[:],
                                            gate_s[:]).then_inc(sMisc, 1)
                        _mv[0] += 1
                        v.wait_ge(sMisc, _mv[0])
                        v.tensor_copy(Sb[:], Sst[:]).then_inc(sA3v, 1)
                else:
                    v.wait_ge(sA3t, T0 + 2)
                    v.tensor_mul(pmb[:], pb[2][:, 0:128], mask_s[:]
                                 ).then_inc(sA3v, 1)
                    v.wait_ge(sA3t, T0 + 5)
                    v.scalar_tensor_tensor(Sst[:], Sst[:], acb[g_][:, 0:1],
                                           pb[3][0:64, 0:128], ALU.mult,
                                           ALU.add).then_inc(sMisc, 1)
                    _mv[0] += 1
                    v.wait_ge(sMisc, _mv[0])
                    v.tensor_copy(Sb[:], Sst[:]).then_inc(sA3v, 1)
                    v.wait_ge(sA3t, T0 + 6)
                    v.scalar_tensor_tensor(Sst[:], Sst[:], acb[g_][:, 1:2],
                                           pb[4][0:64, 0:128], ALU.mult,
                                           ALU.add).then_inc(sMisc, 1)
                    _mv[0] += 1
                    v.wait_ge(sMisc, _mv[0])
                    v.tensor_copy(Sb[:], Sst[:]).then_inc(sA3v, 1)
                    v.tensor_copy(Ob[:, (g_ - 8) * 128:(g_ - 7) * 128],
                                  pb[5][:, 0:128]).then_inc(sA3v, 1)
            for it in range(32):
                v.wait_ge(sOPt, it + 1)
                v.wait_ge(sXop[it % 2], 16 * (it // 2 + 1))
                if it >= 2:
                    v.wait_ge(sStY, 16 * (it - 1))
                v.tensor_add(yst[it % 2][:], xo[it % 2][:], pb[6 + it % 2][:]
                             ).then_inc(sOPv, 1)
            v.tensor_copy(tbs[0:1, 0:2], yst[1][0:1, 0:2]).then_inc(sOPv, 1)
            for ec in range(NEC):
                bA = 0 if ec % 2 == 0 else 4
                if ec == GUARD_QT_EC:
                    v.wait_ge(sStq, 16 * 49)
                if ec == GUARD_YS_EC:
                    v.wait_ge(sStY, 16 * 32)
                v.wait_ge(sFUt, 2 * ec + 2)
                v.wait_ge(sFUs, ec + 1)
                v.tensor_mul(hbuf[:, ec * TOWN:ec * TOWN + 512],
                             sgf[:, 0:512], pb[bA + 2][:])
                v.tensor_mul(hbuf[:, ec * TOWN + 512:(ec + 1) * TOWN],
                             sgf[:, 512:1024], pb[bA + 3][:]).then_inc(sFUv, 1)
            for dc in range(NDC):
                bY = 0 if dc % 2 == 0 else 2
                if dc >= 2:
                    v.wait_ge(sSto, 16 * (dc - 1))
                v.wait_ge(sFDt, 2 * dc + 2)
                v.wait_ge(sYr[dc % 2], 16 * (dc // 2 + 1))
                v.scalar_tensor_tensor(ostage[dc % 2][:, 0:512], pb[bY][:],
                                       colbuf[:, 4:5], ybuf[dc % 2][:, 0:512],
                                       ALU.mult, ALU.add)
                v.scalar_tensor_tensor(ostage[dc % 2][:, 512:1024], pb[bY + 1][:],
                                       colbuf[:, 4:5], ybuf[dc % 2][:, 512:1024],
                                       ALU.mult, ALU.add).then_inc(sFDv, 1)
            v.tensor_copy(tbs[0:1, 0:2], ostage[1][0:1, 0:2]).then_inc(sFDv, 1)

        @blk.scalar
        def _(s):
            for t in range(1, 3 * NQT, 2):
                m, i = divmod(t, NQT)
                if t >= 4:
                    s.wait_ge(sVq1, t - 3)
                s.dma_start(out=qraw[t % 4][:],
                            in_=raws[m].ap()[:, i * 1024:(i + 1) * 1024]
                            ).then_inc(sQr[t % 4], 16)
            for sl in range(NSLAB):
                s.wait_ge(sGx, 16 * (sl + 1))
                for i in range(NDC):
                    if sl * NDC + i >= 2:
                        pi = sl * NDC + i - 2
                        psl, pch = divmod(pi, NDC)
                        s.wait_ge(sLNt, 17 * psl + pch + 1)
                    s.activation(x2t[i % 2][:], xq[:, i * 256:(i + 1) * 256],
                                 AF.Square).then_inc(sLNs, 1)
                s.wait_ge(sLNv, 3 * sl + 1)
                s.activation(varr[:], varr[:], AF.Ln).then_inc(sMiscS, 1)
                _ms[0] += 1
                s.wait_ge(sMiscS, _ms[0])
                s.activation(rstdr[:], varr[:], AF.Exp, scale=-0.5
                             ).then_inc(sLNs, 1)
            s.wait_ge(sVq1, 3 * NQT)
            for t in range(1, 3 * NQT, 2):
                m, i = divmod(t, NQT)
                if t >= 4:
                    s.wait_ge(sVq2, t - 1)
                s.dma_start(out=qraw[t % 4][:],
                            in_=raws[m].ap()[:, i * 1024:(i + 1) * 1024]
                            ).then_inc(sQr[t % 4], 16)
            for g_ in range(NG):
                s.wait_ge(sZt, g_ + 1)
                bz = pb[g_ % 2]
                s.activation(sgb[g_][:], bz[:, 0:64], AF.Sigmoid)
                s.activation(sgd[g_][:], bz[:, 64:128], AF.Sigmoid, scale=-1.0
                             ).then_inc(sZs, 1)
            for g_ in range(NG):
                s.activation(sgb[g_][:], sgb[g_][:], AF.Ln)
                s.activation(sgd[g_][:], sgd[g_][:], AF.Ln).then_inc(sA2s, 1)
                s.wait_ge(sA2v, 4 * g_ + 1)
                s.activation(atok[:], laq[:, 0:64], AF.Exp)
                s.activation(entok[:], laq[:, 0:64], AF.Exp, scale=-1.0
                             ).then_inc(sA2s, 1)
                s.wait_ge(sA2v, 4 * g_ + 2)
                s.activation(ectok[:], dfs[:], AF.Exp, scale=-1.0).then_inc(sA2s, 1)
                s.wait_ge(sA2t, 5 * g_ + 3)
                s.activation(acb[g_][:, 0:1], pb[4][0:64, 63:64], AF.Exp)
                s.activation(acb[g_][:, 1:2], pb[4][0:64, 127:128], AF.Exp
                             ).then_inc(sA2s, 1)
            s.wait_ge(sVq2, 2)
            for ec in range(NEC):
                bA = 0 if ec % 2 == 0 else 4
                s.wait_ge(sFUt, 2 * ec + 1)
                if ec >= 1:
                    s.wait_ge(sFUv, ec)
                s.activation(sgf[:, 0:512], pb[bA][:], AF.Silu,
                             scale=colbuf[:, 3:4])
                s.activation(sgf[:, 512:1024], pb[bA + 1][:], AF.Silu,
                             scale=colbuf[:, 3:4]).then_inc(sFUs, 1)

        @blk.tensor
        def _(t):
            t.wait_ge(sLd, 16 * 5)
            t.wait_ge(sGc, 16 * 3)
            t.wait_ge(sVq1, 3 * NQT + 3)
            for m in range(3):
                t.matmul(pb[6][0:1, m:m + 1], scol[:, m:m + 1], onesc_f[:],
                         start=True, stop=True).then_inc(sTq, 1)
            for sl in range(NSLAB):
                if sl >= 1:
                    t.wait_ge(sLNv, 3 * (sl - 1) + 1)
                for i in range(NDC):
                    t.wait_ge(sLNs, 17 * sl + i + 1)
                    t.matmul(pb[0][0:1, 0:256], onesc_b[:],
                             xq[:, i * 256:(i + 1) * 256],
                             start=(i == 0), stop=(i == NDC - 1))
                    t.matmul(pb[1][0:1, 0:256], onesc_b[:], x2t[i % 2][:],
                             start=(i == 0), stop=(i == NDC - 1)
                             ).then_inc(sLNt, 1)
                t.wait_ge(sLNv, 3 * sl + 2)
                if sl >= 1:
                    t.wait_ge(sLNv, 3 * (sl - 1) + 3)
                t.matmul(pb[2][:, 0:256], onesr_b[:], rstdb[:], start=True, stop=True)
                t.matmul(pb[3][:, 0:256], onesr_b[:], mrb[:], start=True, stop=True
                         ).then_inc(sLNt, 1)
            t.wait_ge(sVq2, 1)
            t.matmul(pb[6][:, 0:6], onesr_f[:], row6[:], start=True, stop=True
                     ).then_inc(sTb, 1)
            t.wait_ge(sZv, 1)
            t.wait_ge(sLNv, 24)
            for g_ in range(NG):
                xn_src = xn_pri if g_ < 8 else xn_own
                goff = (g_ % 8) * 128
                if g_ >= 2:
                    t.wait_ge(sZs, g_ - 1)
                    t.wait_ge(sZv, g_)
                bz = pb[g_ % 2]
                t.matmul(bz[:, 0:384], onesr_b[:], brow_b[:], start=True, stop=False)
                for i in range(NDC):
                    ins = t.matmul(bz[:, 0:384],
                                   xn_src[:, i * TOWN + goff:i * TOWN + goff + 128],
                                   wz_b[:, i * 384:(i + 1) * 384],
                                   start=False, stop=(i == NDC - 1))
                ins.then_inc(sZt, 1)
            t.wait_ge(sVq2, 2)
            for g_ in range(NG):
                t.wait_ge(sA2s, 4 * g_ + 1)
                t.matmul(pb[2][:, 0:64], tri_s[:], sgb[g_][:], start=True, stop=True
                         ).then_inc(sA2t, 1)
                t.wait_ge(sA2v, 4 * g_ + 1)
                t.matmul(pb[3][:, 0:64], gsel_s[:], laq[:, 0:64], start=True,
                         stop=True).then_inc(sA2t, 1)
                t.transpose(pb[4][:, 0:128], laq[:], iden_s[:]).then_inc(sA2t, 1)
                t.wait_ge(sA2v, 4 * g_ + 2)
                t.transpose(pb[5][:, 0:128], qtk[:], iden_s[:]).then_inc(sA2t, 1)
                t.wait_ge(sA2v, 4 * g_ + 3)
                t.transpose(pb[6][:, 0:128], ktk[:], iden_s[:]).then_inc(sA2t, 1)
            t.wait_ge(sA2v, 4 * NG)
            t.wait_ge(sA2s, 4 * NG)
            for g_ in range(NG):
                T0, V0 = Tst[g_], Vst[g_]
                t.wait_ge(sA3v, V0)
                t.matmul(pb[3][0:64, 0:128], khat[g_][0:64, :], vbuf[g_][0:64, :],
                         start=True, stop=True).then_inc(sA3t, 1)
                if g_ < 8:
                    t.matmul(pb[4][0:64, 0:128], khat[g_][64:128, :],
                             vbuf[g_][64:128, :], start=True, stop=True
                             ).then_inc(sA3t, 1)
                else:
                    t.matmul(pb[2][:, 0:128], ktil[g_][:], qtl[g_ - 8][:],
                             start=True, stop=True).then_inc(sA3t, 1)
                    t.matmul(pb[4][0:64, 0:128], khat[g_][64:128, :],
                             vbuf[g_][64:128, :], start=True, stop=True
                             ).then_inc(sA3t, 1)
                    t.wait_ge(sA3v, V0 + 1)
                    t.matmul(pb[5][:, 0:128], vbuf[g_][:], pmb[:],
                             start=True, stop=False).then_inc(sA3t, 1)
                    t.matmul(pb[5][:, 0:64], Sb[:], qtl[g_ - 8][:, 0:64],
                             start=False, stop=False).then_inc(sA3t, 1)
                    t.wait_ge(sA3v, V0 + 2)
                    t.matmul(pb[5][:, 64:128], Sb[:], qtl[g_ - 8][:, 64:128],
                             start=False, stop=True).then_inc(sA3t, 1)
            t.wait_ge(sA3v, A3V_TOT)
            for it in range(32):
                half, dc = divmod(it, NDC)
                if it >= 2:
                    t.wait_ge(sOPv, it)
                t.matmul(pb[6 + it % 2][:], wo_b[:, dc * 128:(dc + 1) * 128],
                         Ob[:, half * 512:(half + 1) * 512],
                         start=True, stop=True).then_inc(sOPt, 1)
            t.wait_ge(sCCu, 1)
            t.wait_ge(sCCg, 1)
            t.wait_ge(sOPv, 32)
            for ec in range(NEC):
                t.wait_ge(sFW[ec % 2], 32 * (ec // 2 + 1))
                if ec >= 2:
                    t.wait_ge(sFUv, ec - 1)
                    t.wait_ge(sFUs, ec - 1)
                bA = 0 if ec % 2 == 0 else 4
                for i in range(NDC):
                    lu = bufU[ec % 2][:, i * 128:(i + 1) * 128]
                    t.matmul(pb[bA][:], lu, xn_own[:, i * TOWN:i * TOWN + 512],
                             start=(i == 0), stop=(i == NDC - 1))
                    ins = t.matmul(pb[bA + 1][:], lu,
                                   xn_own[:, i * TOWN + 512:(i + 1) * TOWN],
                                   start=(i == 0), stop=(i == NDC - 1))
                ins.then_inc(sFUt, 1)
                for i in range(NDC):
                    lg = bufG[ec % 2][:, i * 128:(i + 1) * 128]
                    t.matmul(pb[bA + 2][:], lg, xn_own[:, i * TOWN:i * TOWN + 512],
                             start=(i == 0), stop=(i == NDC - 1))
                    ins = t.matmul(pb[bA + 3][:], lg,
                                   xn_own[:, i * TOWN + 512:(i + 1) * TOWN],
                                   start=(i == 0), stop=(i == NDC - 1))
                ins.then_inc(sFUt, 1)
            t.wait_ge(sCCd, 1)
            t.wait_ge(sFUv, NEC)
            for dc in range(NDC):
                bY = 0 if dc % 2 == 0 else 2
                if dc >= 1:
                    t.wait_ge(sFDv, dc)
                t.wait_ge(sFW2[0], 16 * (dc + 1))
                for ec in range(32):
                    ld = bufD[0][:, ec * 128:(ec + 1) * 128]
                    t.matmul(pb[bY][:], ld, hbuf[:, ec * TOWN:ec * TOWN + 512],
                             start=(ec == 0), stop=False)
                    ins = t.matmul(pb[bY + 1][:], ld,
                                   hbuf[:, ec * TOWN + 512:(ec + 1) * TOWN],
                                   start=(ec == 0), stop=False)
                ins.then_inc(sFDt, 1)
                t.wait_ge(sFW2[1], 16 * (dc + 1))
                for ec in range(32, NEC):
                    ld = bufD[1][:, (ec - 32) * 128:(ec - 31) * 128]
                    t.matmul(pb[bY][:], ld, hbuf[:, ec * TOWN:ec * TOWN + 512],
                             start=False, stop=(ec == NEC - 1))
                    ins = t.matmul(pb[bY + 1][:], ld,
                                   hbuf[:, ec * TOWN + 512:(ec + 1) * TOWN],
                                   start=False, stop=(ec == NEC - 1))
                ins.then_inc(sFDt, 1)

    return nc


def _pretile_updown(W, kind):
    if kind == "up":
        P = W.reshape(64, 128, 16, 128).transpose(3, 0, 2, 1)
    else:
        P = W.reshape(16, 128, 64, 128).transpose(3, 0, 2, 1)
    return np.ascontiguousarray(P.reshape(128, 131072))


def _consts():
    tri = np.zeros((128, 128), np.float32)
    gsel = np.zeros((128, 128), np.float32)
    mask = np.zeros((128, 128), np.float32)
    for u in range(128):
        for t in range(128):
            if u // 64 == t // 64:
                if u <= t:
                    tri[u, t] = 1.0
                    mask[u, t] = 1.0
        gsel[(u // 64) * 64 + 63, u] = 1.0
    iden = np.eye(128, dtype=np.float32)
    return tri, gsel, mask, iden


_CACHE = {}


def kernel(L0_attn_x, L0_attn_Wq, L0_attn_Wk, L0_attn_Wv, L0_attn_Wo,
           L0_attn_Wb, L0_attn_bb, L0_attn_Wd, L0_attn_bd,
           L0_ffn_W_up, L0_ffn_W_gate, L0_ffn_W_down):
    x = np.asarray(L0_attn_x, np.float32)
    tri, gsel, mask, iden = _consts()
    wzc = np.concatenate([np.asarray(L0_attn_Wb, np.float32),
                          np.asarray(L0_attn_Wd, np.float32),
                          np.asarray(L0_attn_Wv, np.float32),
                          np.asarray(L0_attn_Wq, np.float32),
                          np.asarray(L0_attn_Wk, np.float32)], axis=1)
    wzt = np.ascontiguousarray(
        wzc.reshape(16, 128, 384).transpose(1, 0, 2).reshape(128, 6144))
    browc = np.zeros((1, 384), np.float32)
    browc[0, 0:64] = np.asarray(L0_attn_bb, np.float32)
    browc[0, 64:128] = np.asarray(L0_attn_bd, np.float32)
    wo = np.ascontiguousarray(np.asarray(L0_attn_Wo, np.float32))

    Pu = _pretile_updown(np.asarray(L0_ffn_W_up, np.float32), "up")
    Pg = _pretile_updown(np.asarray(L0_ffn_W_gate, np.float32), "up")
    Pd = _pretile_updown(np.asarray(L0_ffn_W_down, np.float32), "down")

    if "nc" not in _CACHE:
        _CACHE["nc"] = build_program(debug=False)
    nc = _CACHE["nc"]

    in_maps = []
    for c in range(N_CORES):
        b, h = divmod(c, 2)
        xT = np.ascontiguousarray(x[b].T)
        r0, r1 = 16 * c, 16 * (c + 1)

        def slabtile(a):
            return np.ascontiguousarray(
                a.reshape(16, 128, 4, 256).transpose(1, 2, 0, 3).reshape(128, 16384))
        xow = xT[:, h * 1024:(h + 1) * 1024]
        in_maps.append(dict(
            x_own=np.ascontiguousarray(xow),
            xt_own=slabtile(xow),
            xt_pri=slabtile(xT[:, 0:1024]),
            gatec=np.full((64, 1), float(h), np.float32),
            wz=wzt, brow=browc, wo=wo,
            tri=tri, gsel=gsel, maskc=mask, iden=iden,
            raw_u=np.ascontiguousarray(Pu[r0:r1].reshape(128, 16384)),
            raw_g=np.ascontiguousarray(Pg[r0:r1].reshape(128, 16384)),
            raw_d=np.ascontiguousarray(Pd[r0:r1].reshape(128, 16384)),
        ))

    res = run_bass_kernel_spmd(nc, in_maps, core_ids=list(range(N_CORES)))

    y = np.empty((4, 2048, 2048), np.float32)
    for c in range(N_CORES):
        b, h = divmod(c, 2)
        y[b, h * 1024:(h + 1) * 1024, :] = res.results[c]["out"].T
    return y
